# version 24 (final)
# speedup vs baseline: 1.1007x; 1.0421x over previous
import numpy as np
import ml_dtypes
import concourse.bass as bass
import concourse.mybir as mybir
from concourse.bass_utils import run_bass_kernel_spmd
from contextlib import ExitStack

F32 = mybir.dt.float32
BF16 = mybir.dt.bfloat16
AF = mybir.ActivationFunctionType
ALU = mybir.AluOpType
AX = mybir.AxisListType.X
KB = 1024
ENGINES = ("tensor", "vector", "scalar", "gpsimd", "sync")


def make_cfg(T=4096):
    c = {"T": T, "D": 1024, "DC": 8, "H": 16, "HP": 8, "HD": 64, "F": 2048,
         "FC": 16, "E": 8}
    c["TO"] = T // 2
    c["C"] = T // 64
    c["NXC"] = T // 128
    c["TB2"] = min(512, T)
    c["NTB2"] = T // c["TB2"]
    c["KQ"] = min(512, T)
    c["NPASS"] = T // c["KQ"]
    c["NKT"] = c["KQ"] // 128
    c["QB"] = min(512, c["TO"])
    c["NQB"] = c["TO"] // c["QB"]
    c["TB5"] = min(512, c["TO"])
    c["NTB5"] = c["TO"] // c["TB5"]
    return c


def build_program(cfg):
    T, TO, D, DC, H, HP, HD, C, F, FC, E = (
        cfg["T"], cfg["TO"], cfg["D"], cfg["DC"], cfg["H"], cfg["HP"],
        cfg["HD"], cfg["C"], cfg["F"], cfg["FC"], cfg["E"])
    TB2, NTB2, KQ, NPASS, NKT, QB, NQB, TB5, NTB5, NXC = (
        cfg["TB2"], cfg["NTB2"], cfg["KQ"], cfg["NPASS"], cfg["NKT"],
        cfg["QB"], cfg["NQB"], cfg["TB5"], cfg["NTB5"], cfg["NXC"])
    NT16 = TO // 128
    assert C <= 64 and 2 * NXC <= 64 and KQ <= 512 and QB <= 512

    nc = bass.Bass("TRN2", target_bir_lowering=False, debug=False)
    stack = ExitStack()
    Q = {e: [] for e in ENGINES}
    ctrs = {}

    def cc(name):
        if name not in ctrs:
            ctrs[name] = [stack.enter_context(nc.semaphore(name)), 0]
        return ctrs[name]

    def st(e, fn, inc=None):
        if inc is None:
            Q[e].append(fn)
            return None
        c = cc(inc)
        c[1] += 1

        def g(eng, fn=fn, sem=c[0]):
            fn(eng).then_inc(sem, 1)
        Q[e].append(g)
        return c[1]

    def wt(e, name, val=None):
        c = cc(name)
        v = c[1] if val is None else val
        Q[e].append(lambda eng, s=c[0], v=v: eng.wait_ge(s, v))

    def dma(e, dst, src, sname):
        c = cc(sname)
        c[1] += 16

        def g(eng, dst=dst, src=src, s=c[0]):
            eng.dma_start(dst, src).then_inc(s, 16)
        Q[e].append(g)
        return c[1]

    def bar():
        for e in ENGINES:
            for cn in ("cT", "cV", "cS"):
                wt(e, cn)
        for nm in list(ctrs):
            if nm.startswith("d"):
                wt("sync", nm)
                wt("gpsimd", nm)

    def mm(psv, lh, rh, first=True, last=True, inc=None):
        return st("tensor",
                  lambda t, o=psv, a=lh, b=rh, f=first, l=last:
                  t.matmul(o, a, b, start=f, stop=l, skip_group_check=True),
                  inc=inc)

    def din(name, shape, dt=F32):
        return nc.dram_tensor(name, shape, dt, kind="ExternalInput").ap()

    x_h = din("x", [T, D])
    wq_h = din("wq", [D, D])
    wk_h = din("wk", [D, D])
    wv_h = din("wv", [D, D])
    wo_h = din("wo_r", [128, HP, D])
    bq_h = din("bq", [128, DC])
    bk_h = din("bk", [128, DC])
    bvr_h = din("bv_row", [1, D])
    bo_h = din("bo", [128, DC])
    wiq_h = din("wiq", [D, D])
    wik_h = din("wik", [D, D])
    wiv_h = din("wiv", [D, D])
    biq_h = din("biq", [128, DC])
    bik_h = din("bik", [128, DC])
    biqh_h = din("biq8h", [64, H])
    bikh_h = din("bikh", [64, H])
    bivr_h = din("biv_row", [1, D])
    wom_h = din("wo_mha_r", [128, HP, D])
    bom_h = din("bo_mha", [128, DC])
    ln1g_h = din("ln1g", [128, DC])
    ln1b_h = din("ln1b", [128, DC])
    ln2g_h = din("ln2g", [128, DC])
    ln2b_h = din("ln2b", [128, DC])
    wgrp_h = din("wgrp", [D, 4])
    bgrp_h = din("bgrp_row", [1, 4])
    wexp_h = din("wexp", [D, E])
    bexp_h = din("bexp_row", [1, E])
    w1_h = din("w1", [E, D, F])
    b1_h = din("b1_r", [E, 128, FC])
    w2_h = din("w2", [E, F, D])
    b2_h = din("b2", [E, D])
    agg_h = din("aggA", [T, C])
    mask_h = din("maskq", [128, T])
    sel2_h = din("sel2", [128, 2])
    sel2t_h = din("sel2t", [2, 128])
    sel16_h = din("sel16", [16, H * 64])
    id_h = din("ident", [128, 128])
    onesr_h = din("ones_row", [1, 128])
    sel8_h = din("sel8", [E, E * 128])
    ones128_h = din("ones_col", [128, 1])
    out_h = nc.dram_tensor("out", [TO, D], F32, kind="ExternalOutput").ap()
    x1sp = nc.dram_tensor("x1spill", [128, DC, T], BF16, kind="Internal").ap()
    ksp = nc.dram_tensor("kspill", [65, H, T], BF16, kind="Internal").ap()
    vsp = nc.dram_tensor("vspill", [128, NPASS, NKT, H, 65], BF16,
                         kind="Internal").ap()

    def sb(name, shape, dt, off):
        return nc.alloc_sbuf_tensor_at(name, shape, dt, offset=off)

    BASE = 16512
    A0 = BASE
    B0 = A0 + 64 * KB
    C0 = B0 + 64 * KB
    D0 = C0 + 32 * KB
    U0 = D0 + 16000
    x_fm = sb("x_fm", [128, DC, T], BF16, A0)
    o_acc = sb("o_acc", [65, H, TO], BF16, A0)
    w1b = [sb("w1b0", [128, DC, F], BF16, A0),
           sb("w1b1", [128, DC, F], BF16, B0 + 32 * KB)]
    w2_cur = sb("w2_cur", [128, FC, D], BF16, A0 + 32 * KB)
    wq_s = sb("wq_s", [128, DC, D], BF16, B0)
    wo_s = sb("wo_s", [128, HP, D], BF16, B0 + 16 * KB)
    maskq_s = sb("maskq_s", [128, T], BF16, B0 + 48 * KB)
    Qh = sb("Qh", [65, 8, TO], BF16, B0)
    wik_s = sb("wik_s", [128, DC, D], BF16, B0)
    wiv_s = sb("wiv_s", [128, DC, D], BF16, B0 + 16 * KB)
    Khs2 = sb("Khs2", [65, 2, H, KQ], BF16, B0 + 32 * KB)
    Vts2 = sb("Vts2", [128, 2, NKT, H, 65], BF16, U0 + 8 * KB)
    wiq_s = sb("wiq_s", [128, DC, D], BF16, B0 + 32 * KB)
    Kpg = sb("Kpg", [65, 2, 8, KQ], BF16, C0)
    Vpg = sb("Vpg", [128, 2, NKT, 8, 65], BF16, B0 + 48 * KB)
    wom_s = sb("wom_s", [128, HP, D], BF16, B0)
    Wpre = sb("Wpre", [128, DC, TO], BF16, B0 + 32 * KB)
    y_acc = sb("y_acc", [128, DC, TO], BF16, B0)
    sq8 = sb("sq8", [128, 2, TB5], BF16, B0 + 32 * KB)
    st8 = sb("st8", [1, 2, TB5], F32, B0 + 35 * KB)
    strow8 = sb("strow8", [1, 2, TB5], BF16, B0 + 40 * KB)
    xstg = sb("xstg", [128, 2, D], F32, C0)
    z1c = sb("z1c", [128, 2, D], BF16, C0 + 8 * KB)
    xbc = sb("xbc", [128, 2, D], BF16, C0 + 12 * KB)
    stat_c = sb("stat_c", [128, 2, 20], F32, C0 + 16 * KB)
    aggS = sb("aggS", [128, NXC, C], BF16, C0 + 17 * KB)
    brows = sb("brows", [1, 2 * D + 16], BF16, C0 + 22 * KB)
    wvstg = sb("wvstg", [128, 2, D], BF16, C0)
    wstg2 = sb("wstg2", [128, 2, DC, 128], BF16, C0 + 8 * KB)
    summf = sb("summf", [128, DC, 64], BF16, C0 + 12 * KB)
    summ_t = sb("summ_t", [64, D], BF16, C0 + 14 * KB)
    bvb = sb("bvb_s", [128, 512], BF16, C0 + 27 * KB)
    pshat2 = sb("pshat2", [128, 4, TB2], BF16, B0 + 56 * KB)
    Ktab = sb("Ktab", [128, HP, 128], BF16, B0 + 60 * KB)
    Vta = sb("Vta", [128, HP, 64], BF16, B0 + 62 * KB)
    rstdbq = sb("rstdbq", [128, 512], BF16, B0 + 32 * KB)
    ktsq = sb("ktsq", [128, 128], BF16, B0 + 33 * KB)
    sqt2 = sb("sqt2", [128, 512], BF16, B0 + 34 * KB)
    stz = sb("stz", [2, 2, 512], F32, B0 + 36 * KB)
    rz2 = sb("rz2", [2, 2, 512], BF16, B0 + 40 * KB)
    wqsum_row = sb("wqsum_row", [1, D], BF16, B0 + 42 * KB)
    qa2 = sb("qa2", [128, HP, TB2], BF16, C0)
    oblk2 = sb("oblk2", [128, HP, TB2], BF16, C0)
    negmu_row = sb("negmu_row", [1, T], BF16, C0 + 16 * KB)
    rstd_row = sb("rstd_row", [1, T], BF16, C0 + 24 * KB)
    z2q2 = sb("z2q2", [128, 2, DC, 512], BF16, C0)
    wstg4 = sb("wstg4", [128, 4, DC, 128], BF16, C0 + 8 * KB)
    sq4 = sb("sq4", [128, 2, 512], BF16, C0 + 8 * KB)
    st4 = sb("st4", [1, 2, min(512, T)], F32, C0 + 10 * KB)
    sqk4b = sb("sqk4b", [128, 2, 512], BF16, U0 + 4 * KB)
    st4b = sb("st4b", [1, 512], F32, U0 + 6 * KB)
    rstdb_all = sb("rstdb_all", [128, TO], BF16, U0 + 8 * KB)
    x1q = sb("x1q", [128, 2, DC, min(512, TO)], BF16, U0 + 12 * KB)
    negmu_own = sb("negmu_own", [1, TO], BF16, U0 + 28 * KB)
    phat4 = sb("phat4", [128, 4, 1024], BF16, C0 + 16 * KB)
    bias128 = sb("bias128", [128, H], F32, C0 + 12 * KB + 128)
    fact128 = sb("fact128", [128, H], F32, C0 + 13 * KB)
    sqk4 = sb("sqk4", [128, 512], BF16, C0 + 13 * KB + 256)
    sqk4 = sb("sqk4X", [128, 512], BF16, C0 + 4 * KB)
    z5 = sb("z5", [128, DC, TB5], BF16, C0)
    sq5 = sb("sq5", [128, TB5], BF16, C0 + 8 * KB)
    st5 = sb("st5", [1, 2, TB5], F32, C0 + 11 * KB)
    opn = sb("opn", [128, HP, TB5], BF16, C0 + 16 * KB)
    zr16 = sb("zr16", [16, 2, TB5], BF16, B0 + 16 * KB)
    st16 = sb("st16", [16, TB5], F32, B0 + 18 * KB)
    rz16 = sb("rz16", [16, 2, TB5], BF16, B0 + 20 * KB)

    gl_tm = sb("gl_tm", [128, NT16, 12], F32, C0)
    gsc = sb("gsc", [128, NT16, 20], F32, C0 + 1 * KB)
    hsc = sb("hsc", [128, FC, TB5], BF16, C0)
    cbc = sb("cbc", [128, TB5], BF16, C0 + 20 * KB)
    c_fm = sb("c_fm", [E, TO], BF16, C0 + 16 * KB)
    x1o8 = sb("x1o8", [128, DC, TB5], BF16, C0)
    x2f = sb("x2f", [128, DC, TB5], F32, B0 + 44 * KB)
    x2tm = sb("x2tm", [128, 2, D], F32, C0 + 24 * KB)
    Dc = [D0]

    def dal(name, shape, dt):
        t = sb(name, shape, dt, Dc[0])
        nb = int(np.prod(shape[1:])) * (2 if dt == BF16 else 4)
        Dc[0] += (nb + 31) // 32 * 32
        return t

    ident = dal("ident_s", [128, 128], BF16)
    identf = dal("identf_s", [128, 128], F32)
    onesr = dal("onesr_s", [1, 128], BF16)
    ones128 = dal("ones128_s", [128, 1], BF16)
    bq8 = dal("bq8_s", [128, DC], F32)
    bk_s = dal("bk_s", [128, DC], F32)
    bo_s = dal("bo_s", [128, DC], F32)
    biq8 = dal("biq8_s", [128, DC], F32)
    bik_s = dal("bik_s", [128, DC], F32)
    biq8h = dal("biq8h_s", [64, H], F32)
    bikh = dal("bikh_s", [64, H], F32)
    bom_s = dal("bom_s", [128, DC], F32)
    ln1g_s = dal("ln1g_s", [128, DC], F32)
    ln1b_s = dal("ln1b_s", [128, DC], F32)
    ln2g_s = dal("ln2g_s", [128, DC], F32)
    ln2b_s = dal("ln2b_s", [128, DC], F32)
    bivb = dal("bivb_s", [128, D], BF16)
    wgrp_s = dal("wgrp_s", [128, DC, 4], BF16)
    wexp_s = dal("wexp_s", [128, DC, E], BF16)
    bgb = dal("bgb_s", [128, 4], F32)
    beb = dal("beb_s", [128, E], F32)
    b1_s = dal("b1_s", [128, E, FC], F32)
    b2_s = dal("b2_s", [E, D], BF16)
    kn2r = dal("kn2r", [1, H], F32)
    lnkr = dal("lnkr", [1, H], F32)
    knalr = dal("knalr", [1, H], F32)
    knbuf2 = dal("knbuf2", [2, HP, NPASS], F32)
    wiqsum2 = dal("wiqsum2", [1, D], BF16)
    strow5 = dal("strow5", [1, 2, TB5], BF16)
    c_tm = dal("c_tm", [128, NT16, E], BF16)
    sel2_s = dal("sel2_s", [128, 2], BF16)
    sel16 = dal("sel16_s", [16, H, 64], BF16)
    sel2t_s = dal("sel2t_s", [2, 128], BF16)
    kn2c = dal("kn2c", [2, HP], F32)
    lnk2 = dal("lnk2", [2, HP], F32)
    qn2c = dal("qn2c", [2, HP], F32)
    bkn = dal("bkn", [2, HP], F32)
    mt2 = dal("mt2", [2, 4, HP], F32)
    nm2 = dal("nm2", [2, HP], BF16)
    kbq_s = dal("kbq_s", [128, HP], F32)
    sqbq = dal("sqbq", [128, HP], BF16)
    biascb2f = dal("biascb2f", [128, HP], F32)
    bq8b = dal("bq8b", [128, DC], BF16)
    sel8 = dal("sel8_s", [E, E, 128], BF16)
    assert Dc[0] <= U0, Dc[0]
    u_s = sb("u_s", [128, DC, TO], BF16, U0)
    assert U0 + DC * TO * 2 <= 229200

    pA = nc.alloc_psum_tensor("pA", [128, 1024], F32)
    pB = nc.alloc_psum_tensor("pB", [128, 1024], F32)
    pC = nc.alloc_psum_tensor("pC", [128, 512], F32)
    pD = nc.alloc_psum_tensor("pD", [128, 512], F32)
    pT = nc.alloc_psum_tensor("pT", [128, 1024], BF16)
    pT2 = nc.alloc_psum_tensor("pT2", [128, 1024], BF16)
    scs4 = (pA[:, 0:512], pA[:, 512:1024], pB[:, 0:512], pB[:, 512:1024])

    for dst, src in [
        (ident[:], id_h[:]), (onesr[:], onesr_h[:]),
        (ones128[:], ones128_h[:]),
        (maskq_s[:], mask_h[:]),
        (wq_s[:], wq_h.rearrange("(c p) d -> p c d", p=128)),
        (wo_s[:], wo_h[:]),
        (wgrp_s[:], wgrp_h.rearrange("(c p) g -> p c g", p=128)),
        (wexp_s[:], wexp_h.rearrange("(c p) g -> p c g", p=128)),
        (b2_s[:], b2_h[:]),
        (brows[:, 0:D], bvr_h[:]), (brows[:, D:2 * D], bivr_h[:]),
        (brows[:, 2 * D:2 * D + 4], bgrp_h[:]),
        (brows[:, 2 * D + 4:2 * D + 4 + E], bexp_h[:]),
        (aggS[:], agg_h.rearrange("(i p) c -> p i c", p=128)),
        (sel8[:], sel8_h.rearrange("k (e p) -> k e p", p=128)),
        (sel2_s[:], sel2_h[:]), (sel2t_s[:], sel2t_h[:]),
        (sel16[:], sel16_h.rearrange("r (h c) -> r h c", c=64)),
    ]:
        dma("gpsimd", dst, src, "dcst")
    for dst, src in [
        (identf[:], id_h[:]), (bk_s[:], bk_h[:]), (bo_s[:], bo_h[:]),
        (bik_s[:], bik_h[:]), (bom_s[:], bom_h[:]), (ln1g_s[:], ln1g_h[:]),
        (ln1b_s[:], ln1b_h[:]), (ln2g_s[:], ln2g_h[:]),
        (ln2b_s[:], ln2b_h[:]),
        (b1_s[:], b1_h.rearrange("e p f -> p e f")),
        (bq8[:], bq_h[:]), (biq8[:], biq_h[:]),
        (biq8h[:], biqh_h[:]), (bikh[:], bikh_h[:]),
    ]:
        dma("sync", dst, src, "dcs2")
    wt("vector", "dcs2")
    wt("vector", "dcst")
    st("vector", lambda v: v.tensor_scalar_mul(bq8[:], bq8[:], 0.125), "cV")
    wt("vector", "cV")
    st("vector", lambda v: v.tensor_copy(bq8b[:], bq8[:]), "cV")
    st("vector", lambda v: v.tensor_scalar_mul(biq8[:], biq8[:], 0.125), "cV")
    st("vector", lambda v: v.tensor_scalar_mul(biq8h[:], biq8h[:], 0.125),
       "cV")
    wt("tensor", "dcst")
    wt("tensor", "dcs2")
    for j in range(2):
        b0 = 64 * j
        mm(pD[b0:b0 + 64, :], onesr[:, 0:64],
           brows[:, j * 512:(j + 1) * 512], inc="cT")
        wt("vector", "cT")
        st("vector", lambda v, b0=b0: v.tensor_copy(
            bvb[b0:b0 + 64, :], pD[b0:b0 + 64, :]), "cV")
        wt("tensor", "cV")
    for j in range(D // 512):
        mm(pD[:, :], onesr[:], brows[:, D + j * 512:D + (j + 1) * 512],
           inc="cT")
        wt("vector", "cT")
        st("vector", lambda v, j=j: v.tensor_copy(
            bivb[:, j * 512:(j + 1) * 512], pD[:, :]), "cV")
        wt("tensor", "cV")
    mm(pD[:, 0:4], onesr[:], brows[:, 2 * D:2 * D + 4], inc="cT")
    wt("vector", "cT")
    st("vector", lambda v: v.tensor_copy(bgb[:], pD[:, 0:4]), "cV")
    wt("tensor", "cV")
    mm(pD[:, 4:4 + E], onesr[:], brows[:, 2 * D + 4:2 * D + 4 + E], inc="cT")
    wt("vector", "cT")
    st("vector", lambda v: v.tensor_copy(beb[:], pD[:, 4:4 + E]), "cV")
    bar()

    consSV = []
    consT = []
    ptfree = []
    for i in range(NXC):
        sl = i % 2
        dxs = "dx%d" % sl
        qeng = "sync" if i % 2 == 0 else "gpsimd"
        if i >= 2:
            wt(qeng, "cS", consSV[i - 2][0])
            wt(qeng, "cT", consSV[i - 2][1])
        dma(qeng, xstg[:, sl, :],
            x_h[128 * i:128 * (i + 1), :], dxs)
        wt("vector", dxs)
        if i >= 2:
            wt("vector", "cS", consSV[i - 2][0])
        st("vector", lambda v, sl=sl: v.bn_stats(
            stat_c[:, sl, 2:8], xstg[:, sl, 0:512]), "cV")
        st("vector", lambda v, sl=sl: v.bn_stats(
            stat_c[:, sl, 8:14], xstg[:, sl, 512:1024]), "cV")
        wt("vector", "cV")
        st("vector", lambda v, sl=sl: v.bn_aggr(
            stat_c[:, sl, 0:2], stat_c[:, sl, 2:14]), "cV")
        wt("vector", "cV")
        st("vector", lambda v, sl=sl: v.tensor_scalar_add(
            stat_c[:, sl, 1:2], stat_c[:, sl, 1:2], 1e-5), "cV")
        wt("scalar", "cV")
        st("scalar", lambda s, sl=sl: s.activation(
            stat_c[:, sl, 14:15], stat_c[:, sl, 1:2], AF.Ln, bias=0.0,
            scale=1.0), "cS")
        wt("scalar", "cS")
        st("scalar", lambda s, sl=sl: s.activation(
            stat_c[:, sl, 15:16], stat_c[:, sl, 14:15], AF.Exp, bias=0.0,
            scale=-0.5), "cS")
        wt("vector", "cS")
        st("vector", lambda v, sl=sl: v.tensor_scalar(
            stat_c[:, sl, 16:17], stat_c[:, sl, 0:1], stat_c[:, sl, 15:16],
            -1.0, ALU.mult, ALU.mult), "cV")
        wt("vector", "cV")
        wt("scalar", "cV")
        st("scalar", lambda s, sl=sl: s.activation(
            z1c[:, sl, :], xstg[:, sl, :], AF.Identity,
            bias=stat_c[:, sl, 16:17], scale=stat_c[:, sl, 15:16]), "cS")
        nS = cc("cS")[1]
        wt("tensor", "cS", nS)
        for j in range(2):
            mm(pA[0:C, j * 512:(j + 1) * 512], aggS[:, i, :],
               z1c[:, sl, j * 512:(j + 1) * 512],
               first=(i == 0), last=(i == NXC - 1),
               inc="cT" if j == 1 else None)
        for dcx in range(DC):
            k = i * DC + dcx
            if k >= 2:
                wt("tensor", "cV", ptfree[k - 2])
            pTx = pC if dcx % 2 == 0 else pD
            st("tensor", lambda t, sl=sl, dcx=dcx, pTx=pTx: t.transpose(
                pTx[:, 0:128],
                xstg[:, sl, dcx * 128:(dcx + 1) * 128], identf[:]), "cT")
            wt("vector", "cT")
            st("vector", lambda v, i=i, dcx=dcx, pTx=pTx: v.tensor_copy(
                x_fm[:, dcx, i * 128:(i + 1) * 128], pTx[:, 0:128]), "cV")
            ptfree.append(cc("cV")[1])
        consT.append(cc("cT")[1])
        consSV.append((nS, consT[-1]))
    wt("scalar", "cT")
    st("scalar", lambda s: s.activation(
        summ_t[0:C, :], pA[0:C, 0:1024], AF.Identity, bias=0.0, scale=1.0), "cS")
    bar()

    for j in range(DC):
        if j >= 2:
            wt("tensor", "cV")
        pTx = pT if j % 2 == 0 else pT2
        st("tensor", lambda t, j=j, pTx=pTx: t.transpose(
            pTx[:, 0:C],
            summ_t[0:C, j * 128:(j + 1) * 128], ident[0:C, 0:C]), "cT")
        wt("vector", "cT")
        st("vector", lambda v, j=j, pTx=pTx: v.tensor_copy(
            summf[:, j, 0:C], pTx[:, 0:C]), "cV")
    wt("tensor", "cV")
    for dc in range(DC):
        for j in range(2):
            mm(pA[0:1, j * 512:(j + 1) * 512], ones128[:],
               wq_s[:, dc, j * 512:(j + 1) * 512],
               first=(dc == 0), last=(dc == DC - 1),
               inc="cT" if (dc == DC - 1 and j == 1) else None)
    wt("scalar", "cT")
    st("scalar", lambda s: s.activation(
        wqsum_row[:], pA[0:1, 0:1024], AF.Identity, bias=0.0, scale=1.0),
       "cS")
    st("vector", lambda v: v.memset(Ktab[:], 0.0), "cV")
    wt("scalar", "cV")
    wkuse = []
    for p in range(HP):
        sl = p % 2
        if p >= 2:
            wt("gpsimd", "cT", wkuse[p - 2])
        dma("gpsimd", wstg2[:, sl, :, :],
            wk_h[:, 128 * p:128 * (p + 1)].rearrange("(c p) n -> p c n",
                                                     p=128), "dwk%d" % sl)
        wt("tensor", "dwk%d" % sl)
        for dc in range(DC):
            mm(pB[:, 0:C], wstg2[:, sl, dc, :], summf[:, dc, 0:C],
               first=(dc == 0), last=(dc == DC - 1),
               inc="cT" if dc == DC - 1 else None)
        wkuse.append(cc("cT")[1])
        wt("scalar", "cT")
        st("scalar", lambda s, p=p: s.activation(
            Ktab[0:64, p, 0:C], pB[0:64, 0:C], AF.Identity,
            bias=bk_s[0:64, p:p + 1], scale=1.0), "cS")
        st("scalar", lambda s, p=p: s.activation(
            Ktab[64:128, p, 64:64 + C], pB[64:128, 0:C], AF.Identity,
            bias=bk_s[64:128, p:p + 1], scale=1.0), "cS")
        wt("tensor", "cS")
    wvuse = []
    for dc in range(DC):
        sl = dc % 2
        if dc >= 2:
            wt("gpsimd", "cT", wvuse[dc - 2])
        dma("gpsimd", wvstg[:, sl, :], wv_h[128 * dc:128 * (dc + 1), :],
            "dwv%d" % sl)
        wt("tensor", "dwv%d" % sl)
        mm(pA[0:C, 0:512], summf[:, dc, 0:C], wvstg[:, sl, 0:512],
           first=(dc == 0), last=(dc == DC - 1))
        mm(pA[64:64 + C, 512:1024], summf[:, dc, 0:C],
           wvstg[:, sl, 512:1024], first=(dc == 0), last=(dc == DC - 1),
           inc="cT")
        wvuse.append(cc("cT")[1])
    wt("vector", "cT")
    st("vector", lambda v: v.tensor_tensor(
        Vta.ap()[0:C, :, :],
        pA[0:C, 0:512].rearrange("c (h d) -> c h d", d=64),
        bvb[0:C, :].rearrange("c (h d) -> c h d", d=64), ALU.add), "cV")
    st("vector", lambda v: v.tensor_tensor(
        Vta.ap()[64:64 + C, :, :],
        pA[64:64 + C, 512:1024].rearrange("c (h d) -> c h d", d=64),
        bvb[64:64 + C, :].rearrange("c (h d) -> c h d", d=64), ALU.add),
       "cV")
    knredE = []
    for p in range(HP):
        wt("vector", "cT")
        st("vector", lambda v, p=p: v.tensor_mul(
            ktsq[:, :], Ktab[:, p, :], Ktab[:, p, :]), "cV")
        wt("tensor", "cV")
        if p >= 4:
            wt("tensor", "cV", knredE[p - 4])
        ksl_ = scs4[p % 4]
        mm(ksl_[0:2, 0:128], sel2_s[:], ktsq[:, :], inc="cT")
        wt("vector", "cT")
        st("vector", lambda v, p=p, ksl_=ksl_: v.reduce_max(
            kn2c[0:2, p:p + 1], ksl_[0:2, 0:128], axis=AX), "cV")
        knredE.append(cc("cV")[1])
    wt("scalar", "cV")
    st("scalar", lambda s: s.activation(
        lnk2[:], kn2c[:], AF.Ln, bias=0.0, scale=1.0), "cS")
    st("vector", lambda v: v.memset(kbq_s[:], 0.0), "cV")
    wt("tensor", "cV", knredE[-1])
    for p in range(HP):
        mm(pD[0:C, p:p + 1], Ktab[0:64, p, 0:C], bq8b[0:64, p:p + 1],
           first=True, last=True)
        mm(pD[64:64 + C, p:p + 1], Ktab[64:128, p, 64:64 + C],
           bq8b[64:128, p:p + 1], first=True, last=True,
           inc="cT" if p == HP - 1 else None)
    wt("vector", "cT")
    wt("vector", "cV")
    st("vector", lambda v: v.tensor_copy(kbq_s[0:C, :], pD[0:C, 0:8]),
       "cV")
    st("vector", lambda v: v.tensor_copy(
        kbq_s[64:64 + C, :], pD[64:64 + C, 0:8]), "cV")
    st("vector", lambda v: v.tensor_mul(sqbq[:], bq8b[:], bq8b[:]), "cV")
    wt("tensor", "cV")
    mm(pD[0:2, 8:16], sel2_s[:], sqbq[:], inc="cT")
    wt("scalar", "cT")
    st("scalar", lambda s: s.activation(
        mt2[:, 0, :], pD[0:2, 8:16], AF.Ln, bias=0.0, scale=1.0), "cS")
    wt("vector", "cS")
    st("vector", lambda v: v.tensor_add(mt2[:, 1, :], mt2[:, 0, :],
                                        lnk2[:]), "cV")
    wt("scalar", "cV")
    st("scalar", lambda s: s.activation(
        bkn[:], mt2[:, 1, :], AF.Exp, bias=0.0, scale=0.5), "cS")
    bar()

    LNW = min(512, T)
    def ln_rows(src_fm, span):
        NSx = span // LNW
        for sbk in range(NSx):
            ssl = slice(sbk * LNW, sbk * LNW + LNW)
            for dc in range(DC):
                mm(pA[0:1, 0:LNW], ones128[:], src_fm[:, dc, ssl],
                   first=(dc == 0), last=(dc == DC - 1),
                   inc="cT" if dc == DC - 1 else None)
            for dc in range(DC):
                if dc >= 2:
                    wt("vector", "cT")
                st("vector", lambda v, dc=dc, ssl=ssl: v.tensor_mul(
                    sq4[:, dc % 2, 0:LNW], src_fm[:, dc, ssl],
                    src_fm[:, dc, ssl]), "cV")
                wt("tensor", "cV")
                mm(pD[0:1, 0:LNW], ones128[:], sq4[:, dc % 2, 0:LNW],
                   first=(dc == 0), last=(dc == DC - 1), inc="cT")
            wt("vector", "cT")
            st("vector", lambda v, ssl=ssl: v.tensor_scalar_mul(
                negmu_row[:, ssl], pA[0:1, 0:LNW], -1.0 / D), "cV")
            st("vector", lambda v: v.tensor_scalar_mul(
                st4[:, 0, :], pD[0:1, 0:LNW], 1.0 / D), "cV")
            wt("vector", "cV")
            st("vector", lambda v, ssl=ssl: v.tensor_mul(
                st4[:, 1, :], negmu_row[:, ssl], negmu_row[:, ssl]), "cV")
            wt("vector", "cV")
            st("vector", lambda v: v.tensor_sub(
                st4[:, 0, :], st4[:, 0, :], st4[:, 1, :]), "cV")
            wt("vector", "cV")
            st("vector", lambda v: v.tensor_scalar_add(
                st4[:, 0, :], st4[:, 0, :], 1e-5), "cV")
            wt("scalar", "cV")
            st("scalar", lambda s: s.activation(
                st4[:, 1, :], st4[:, 0, :], AF.Ln, bias=0.0, scale=1.0),
               "cS")
            wt("scalar", "cS")
            st("scalar", lambda s, ssl=ssl: s.activation(
                rstd_row[:, ssl], st4[:, 1, :], AF.Exp, bias=0.0,
                scale=-0.5), "cS")
            wt("tensor", "cS")
            wt("vector", "cS")

    ln_rows(x_fm, T)
    bar()
    expE, pvE, maskE, zE, lnE, rexpE, bcE, nrmE, cpE = ({} for _ in range(9))
    qaE, qnmmE, qredE, sttE = {}, {}, {}, {}
    prevWoT = [None]
    for tb in range(NTB2):
        tsl = slice(tb * TB2, (tb + 1) * TB2)
        base = tb * 8
        if tb > 0:
            wt("tensor", "cS", lnE[base - 4])
        mm(scs4[0][:, 0:TB2], onesr[:], rstd_row[:, tsl], inc="cT")
        wt("vector", "cT")
        if tb > 0:
            wt("vector", "cT", prevWoT[0])
        st("vector", lambda v: v.tensor_scalar_mul(
            rstdbq[:, 0:TB2], scs4[0][:, 0:TB2], 0.125), "cV")
        rstdE = cc("cV")[1]
        for pr in range(HP):
            bank = pC if pr % 2 == 0 else pD
            if tb > 0 and pr < 2:
                wt("tensor", "cV", sttE[(tb - 1, 6 + pr)])
            if pr >= 2:
                wt("tensor", "cV", qredE[base + pr - 2])
            for dc in range(DC):
                mm(bank[:, 0:TB2], wq_s[:, dc, 128 * pr:128 * (pr + 1)],
                   x_fm[:, dc, tsl], first=(dc == 0), last=False)
            mm(bank[:, 0:TB2], wqsum_row[:, 128 * pr:128 * (pr + 1)],
               negmu_row[:, tsl], first=False, last=True, inc="cT")
            qammE = cc("cT")[1]
            wt("vector", "cT", qammE)
            if pr == 0:
                wt("vector", "cV", rstdE)
            st("vector", lambda v, pr=pr, bank=bank: v.tensor_tensor(
                qa2[:, pr, :], bank[:, 0:TB2], rstdbq[:, 0:TB2],
                ALU.mult), "cV")
            qaE[base + pr] = cc("cV")[1]
            if pr >= 1:
                wt("vector", "cT", qnmmE[base + pr - 1])
            wt("vector", "cV", qaE[base + pr])
            st("vector", lambda v, pr=pr: v.tensor_mul(
                sqt2[:, 0:TB2], qa2[:, pr, :], qa2[:, pr, :]), "cV")
            wt("tensor", "cV")
            mm(bank[0:2, 0:TB2], sel2_s[:], sqt2[:, 0:TB2], inc="cT")
            qnmmE[base + pr] = cc("cT")[1]
            wt("vector", "cT", qnmmE[base + pr])
            st("vector", lambda v, pr=pr, bank=bank: v.reduce_max(
                qn2c[:, pr:pr + 1], bank[0:2, 0:TB2], axis=AX), "cV")
            qredE[base + pr] = cc("cV")[1]
        wt("scalar", "cV", qredE[base + 7])
        st("scalar", lambda s: s.activation(
            mt2[:, 0, :], qn2c[:], AF.Ln, bias=0.0, scale=1.0), "cS")
        wt("vector", "cS")
        st("vector", lambda v: v.tensor_add(
            mt2[:, 1, :], mt2[:, 0, :], lnk2[:]), "cV")
        wt("scalar", "cV")
        st("scalar", lambda s: s.activation(
            mt2[:, 2, :], mt2[:, 1, :], AF.Exp, bias=0.0, scale=0.5), "cS")
        wt("vector", "cS")
        st("vector", lambda v: v.tensor_add(
            mt2[:, 3, :], mt2[:, 2, :], bkn[:]), "cV")
        wt("vector", "cV")
        st("vector", lambda v: v.tensor_scalar(
            nm2[:], mt2[:, 3, :], -1.0, -0.1, ALU.mult, ALU.add), "cV")
        wt("tensor", "cV")
        if tb > 0:
            wt("tensor", "cS", lnE[base - 1])
        mm(scs4[3][:, 0:8], sel2t_s[:], nm2[:], inc="cT")
        wt("vector", "cT")
        st("vector", lambda v: v.tensor_tensor(
            biascb2f[:], scs4[3][:, 0:8], kbq_s[:], ALU.add), "cV")
        biasE = cc("cV")[1]
        for it in range(10):
            if it < 8:
                i, ai = it, base + it
                s_ = scs4[it % 4]
                if ai - 4 >= 0:
                    wt("tensor", "cS", lnE[ai - 4])
                if i == 0:
                    wt("tensor", "cV", rstdE)
                if i == 3:
                    wt("tensor", "cV", biasE)
                mm(s_[:, 0:TB2], Ktab[:, i, :], qa2[:, i, :], inc="cT")
                scE = cc("cT")[1]
                wt("scalar", "cT", scE)
                if ai - 4 >= 0:
                    wt("scalar", "cT", pvE[ai - 4])
                wt("scalar", "cV", biasE)
                st("scalar", lambda s, i=i, s_=s_: s.activation(
                    pshat2[:, i % 4, :], s_[:, 0:TB2], AF.Exp,
                    bias=biascb2f[:, i:i + 1], scale=1.0), "cS")
                expE[ai] = cc("cS")[1]
                wt("vector", "cS", expE[ai])
                st("vector", lambda v, i=i, tsl=tsl: v.tensor_tensor(
                    pshat2[:, i % 4, :], pshat2[:, i % 4, :],
                    maskq_s[:, tsl], ALU.mult), "cV")
                maskE[ai] = cc("cV")[1]
            if 1 <= it < 9:
                j, aj = it - 1, base + it - 1
                zs_ = scs4[j % 4]
                wt("tensor", "cV", maskE[aj])
                mm(zs_[0:2, 0:TB2], sel2_s[:], pshat2[:, j % 4, :],
                   inc="cT")
                zE[aj] = cc("cT")[1]
                wt("scalar", "cT", zE[aj])
                st("scalar", lambda s, j=j, zs_=zs_: s.activation(
                    stz[:, j % 2, :], zs_[0:2, 0:TB2], AF.Ln,
                    bias=0.0, scale=1.0), "cS")
                lnE[aj] = cc("cS")[1]
                if aj - 2 >= 0:
                    wt("scalar", "cT", bcE[aj - 2])
                wt("scalar", "cS", lnE[aj])
                st("scalar", lambda s, j=j: s.activation(
                    rz2[:, j % 2, :], stz[:, j % 2, :], AF.Exp,
                    bias=0.0, scale=-1.0), "cS")
                rexpE[aj] = cc("cS")[1]
            if it >= 2:
                k, ak = it - 2, base + it - 2
                po = pC if k % 2 == 0 else pD
                wt("tensor", "cS", rexpE[ak])
                if k < 2:
                    wt("tensor", "cV", qredE[base + 6 + k])
                else:
                    wt("tensor", "cV", cpE[ak - 2])
                mm(po[:, 0:TB2], sel2t_s[:], rz2[:, k % 2, :], inc="cT")
                bcE[ak] = cc("cT")[1]
                wt("vector", "cT", bcE[ak])
                st("vector", lambda v, k=k, po=po: v.tensor_tensor(
                    pshat2[:, k % 4, :], pshat2[:, k % 4, :],
                    po[:, 0:TB2], ALU.mult), "cV")
                nrmE[ak] = cc("cV")[1]
                wt("tensor", "cV", nrmE[ak])
                mm(po[0:64, 0:TB2], Vta[0:C, k, :],
                   pshat2[0:C, k % 4, :], first=True, last=True)
                mm(po[64:128, 0:TB2], Vta[64:64 + C, k, :],
                   pshat2[64:64 + C, k % 4, :], first=True, last=True,
                   inc="cT")
                pvE[ak] = cc("cT")[1]
                wt("vector", "cT", pvE[ak])
                st("vector", lambda v, k=k, po=po: v.tensor_copy(
                    oblk2[:, k, :], po[:, 0:TB2]), "cV")
                cpE[ak] = cc("cV")[1]
        for dc in range(DC):
            pw = pC if dc % 2 == 0 else pD
            if dc == 0:
                wt("tensor", "cV", cpE[base + 7])
            elif dc >= 2:
                wt("tensor", "cV", sttE[(tb, dc - 2)])
            for pr in range(HP):
                mm(pw[:, 0:TB2], wo_s[:, pr, 128 * dc:128 * (dc + 1)],
                   oblk2[:, pr, :], first=(pr == 0), last=(pr == HP - 1),
                   inc="cT" if pr == HP - 1 else None)
            woE = cc("cT")[1]
            if dc == DC - 1:
                prevWoT[0] = woE
            wt("vector", "cT", woE)
            st("vector", lambda v, dc=dc, tsl=tsl, pw=pw:
               v.scalar_tensor_tensor(
                   x_fm[:, dc, tsl], pw[:, 0:TB2], bo_s[:, dc:dc + 1],
                   x_fm[:, dc, tsl], ALU.add, ALU.add), "cV")
            sttE[(tb, dc)] = cc("cV")[1]
    bar()

    dma("sync", x1sp[:], x_fm[:], "dsp")
    ln_rows(x_fm, T)
    bar()

    wt("sync", "dsp")
    wt("vector", "dsp")
    st("vector", lambda v: v.memset(o_acc[:], 0.0), "cV")
    st("vector", lambda v: v.memset(Khs2[64:65, :, :, :], 1.0), "cV")
    st("vector", lambda v: v.memset(Vts2[:, :, :, :, 64:65], 1.0), "cV")
    dma("gpsimd", wik_s[:], wik_h.rearrange("(c p) n -> p c n", p=128),
        "dwik")
    dma("gpsimd", wiv_s[:], wiv_h.rearrange("(c p) n -> p c n", p=128),
        "dwiv")
    wt("tensor", "dwik")
    wt("tensor", "dwiv")
    wt("tensor", "cV")
    wt("scalar", "cV")
    sqE = {}
    vevE = {}
    kspD = {}
    vspD = {}
    kevL = {}
    knrE = {}
    knmE = {}
    vmmT = {}
    for pp in range(NPASS):
        psl = pp % 2
        z2q = z2q2[:, psl]
        Khs = Khs2[:, psl]
        Vts = Vts2[:, psl]
        ksl = slice(pp * KQ, (pp + 1) * KQ)
        drs = "dr%d" % psl
        if pp >= 2:
            wt("sync", "cT", vmmT[pp - 2])
        dma("sync", z2q[:, :, :], x1sp[:, :, ksl], drs)
        wt("vector", drs)
        if pp >= 1:
            wt("tensor", "cV")
        mm(pD[:, 0:KQ], onesr[:], negmu_row[:, ksl], inc="cT")
        wt("vector", "cT")
        for dc in range(DC):
            st("vector", lambda v, dc=dc, z2q=z2q: v.tensor_tensor(
                z2q[:, dc, :], z2q[:, dc, :], pD[:, 0:KQ], ALU.add), "cV")
        wt("tensor", "cV")
        mm(pD[:, 0:KQ], onesr[:], rstd_row[:, ksl], inc="cT")
        wt("vector", "cT")
        for dc in range(DC):
            st("vector", lambda v, dc=dc, z2q=z2q: v.tensor_tensor(
                z2q[:, dc, :], z2q[:, dc, :], pD[:, 0:KQ], ALU.mult), "cV")
        wt("tensor", "cV")
        for pit in range(HP + 1):
            if pit < HP:
                p = pit
                ps = scs4[p % 4]
                if (pp, p - 4) in sqE:
                    wt("tensor", "cS", sqE[(pp, p - 4)])
                elif (pp - 1, p + 4) in sqE:
                    wt("tensor", "cS", sqE[(pp - 1, p + 4)])
                for dc in range(DC):
                    mm(ps[:, 0:KQ], wik_s[:, dc, 128 * p:128 * (p + 1)],
                       z2q[:, dc, :], first=(dc == 0), last=(dc == DC - 1),
                       inc="cT" if dc == DC - 1 else None)
                wt("scalar", "cT")
                if pp >= 2 and p == 0:
                    wt("scalar", "dksA", kspD[pp - 2])
                st("scalar", lambda s, p=p, ps=ps, Khs=Khs: s.activation(
                    Khs[0:64, 2 * p, :], ps[0:64, 0:KQ], AF.Identity,
                    bias=bik_s[0:64, p:p + 1], scale=1.0), "cS")
                st("scalar", lambda s, p=p, ps=ps, Khs=Khs: s.activation(
                    Khs[0:64, 2 * p + 1, :], ps[64:128, 0:KQ], AF.Identity,
                    bias=bik_s[64:128, p:p + 1], scale=1.0), "cS")
                kevL[pp] = cc("cS")[1]
                wt("scalar", "cS", kevL[pp])
                if (pp, p - 2) in knmE:
                    wt("scalar", "cT", knmE[(pp, p - 2)])
                elif (pp - 1, p + 6) in knmE:
                    wt("scalar", "cT", knmE[(pp - 1, p + 6)])
                st("scalar", lambda s, p=p, Khs=Khs: s.activation(
                    sqk4b[0:64, p % 2, :], Khs[0:64, 2 * p, :], AF.Square,
                    bias=0.0, scale=1.0), "cS")
                st("scalar", lambda s, p=p, Khs=Khs: s.activation(
                    sqk4b[64:128, p % 2, :], Khs[0:64, 2 * p + 1, :],
                    AF.Square, bias=0.0, scale=1.0), "cS")
                sqE[(pp, p)] = cc("cS")[1]
            if pit >= 1:
                p = pit - 1
                wt("tensor", "cS", sqE[(pp, p)])
                pn = pC if p % 2 == 0 else pD
                if (pp, p - 2) in knrE:
                    wt("tensor", "cV", knrE[(pp, p - 2)])
                elif (pp - 1, p + 6) in knrE:
                    wt("tensor", "cV", knrE[(pp - 1, p + 6)])
                mm(pn[0:2, 0:KQ], sel2_s[:], sqk4b[:, p % 2, :], inc="cT")
                knmE[(pp, p)] = cc("cT")[1]
                wt("vector", "cT", knmE[(pp, p)])
                st("vector", lambda v, p=p, pp=pp, pn=pn: v.reduce_max(
                    knbuf2[:, p:p + 1, pp:pp + 1], pn[0:2, 0:KQ], axis=AX),
                   "cV")
                knrE[(pp, p)] = cc("cV")[1]
        wt("tensor", "cS")
        for kt in range(NKT):
            px = pA if kt % 2 == 0 else pB
            if (pp, kt - 2) in vevE:
                wt("tensor", "cV", vevE[(pp, kt - 2)])
            elif (pp - 1, kt + 2) in vevE:
                wt("tensor", "cV", vevE[(pp - 1, kt + 2)])
            for dc in range(DC):
                for jh in range(2):
                    mm(px[:, jh * 512:(jh + 1) * 512],
                       z2q[:, dc, 128 * kt:128 * (kt + 1)],
                       wiv_s[:, dc, jh * 512:(jh + 1) * 512],
                       first=(dc == 0), last=(dc == DC - 1),
                       inc="cT" if (dc == DC - 1 and jh == 1) else None)
            wt("vector", "cT")
            if pp >= 2 and kt == 0:
                wt("vector", "dvsA", vspD[pp - 2])
            st("vector", lambda v, kt=kt, px=px, Vts=Vts: v.tensor_tensor(
                Vts[:, kt, :, 0:64],
                px[:, 0:1024].rearrange("p (h d) -> p h d", d=64),
                bivb[:, :].rearrange("p (h d) -> p h d", d=64), ALU.add),
               "cV")
            vevE[(pp, kt)] = cc("cV")[1]
        vmmT[pp] = cc("cT")[1]
        wt("gpsimd", "cS", kevL[pp])
        kspD[pp] = dma("gpsimd", ksp[:, :, ksl], Khs[:, :, :], "dksA")
        wt("gpsimd", "cV", vevE[(pp, NKT - 1)])
        vspD[pp] = dma("gpsimd", vsp[:, pp, :, :, :], Vts[:, :, :, :],
                       "dvsA")
    wt("vector", "cV")
    st("vector", lambda v: v.reduce_max(
        kn2c.ap().rearrange("a (h o) -> a h o", o=1), knbuf2[:, :, :],
        axis=AX), "cV")
    wt("scalar", "cV")
    st("scalar", lambda s: s.activation(
        mt2[:, 0, :], kn2c[:], AF.Ln, bias=0.0, scale=1.0), "cS")
    wt("scalar", "cS")
    st("scalar", lambda s: s.activation(
        mt2[:, 1, :], mt2[:, 0, :], AF.Exp, bias=0.0, scale=0.5), "cS")
    wt("vector", "cS")
    st("vector", lambda v: v.tensor_scalar_mul(mt2[:, 2, :], mt2[:, 1, :],
                                               -1.0), "cV")
    wt("sync", "cV")
    for par in range(2):
        c_ = cc("dkn")
        c_[1] += 16
        Q["sync"].append(
            lambda eng, par=par, s=c_[0]: eng.dma_start(
                knalr.ap().rearrange("a (h two) -> a two h",
                                     two=2)[:, par, :],
                mt2[par:par + 1, 2, :],
                allow_slow_non_contiguous=True).then_inc(s, 16))
    bar()

    dma("gpsimd", wiq_s[:], wiq_h.rearrange("(c p) n -> p c n", p=128),
        "dwiq")
    wt("vector", "dvsA", vspD[NPASS - 1])
    t4w = min(512, TO)
    for t4 in range(TO // t4w):
        mm(scs4[t4 % 4][:, 0:t4w], onesr[:],
           rstd_row[:, t4 * t4w:(t4 + 1) * t4w], inc="cT")
        wt("vector", "cT")
        st("vector", lambda v, t4=t4: v.tensor_scalar_mul(
            rstdb_all[:, t4 * t4w:(t4 + 1) * t4w],
            scs4[t4 % 4][:, 0:t4w], 0.125), "cV")
    wt("tensor", "cV")
    wt("tensor", "dwiq")
    for dc in range(DC):
        for j in range(2):
            mm(pA[0:1, j * 512:(j + 1) * 512], ones128[:],
               wiq_s[:, dc, j * 512:(j + 1) * 512],
               first=(dc == 0), last=(dc == DC - 1),
               inc="cT" if (dc == DC - 1 and j == 1) else None)
    wt("scalar", "cT")
    st("scalar", lambda s: s.activation(
        wiqsum2[:], pA[0:1, 0:1024], AF.Identity, bias=0.0, scale=1.0),
       "cS")
    wt("tensor", "cS")
    st("vector", lambda v: v.tensor_copy(negmu_own[:], negmu_row[:, 0:TO]),
       "cV")
    wt("tensor", "cV")
    wt("scalar", "cV")

    TQB = min(512, TO)
    NTQ = TO // TQB
    exp_done = []
    oev4 = []
    attT = {}
    qevE = {}
    lnqE = {}
    kldD = {}
    vldD = {}
    qprojT = {}
    pv_all = {}
    gcount = [0]
    qpend = [None]

    def emit_qnorm(p, pq, tsl, idx):
        for r in range(2):
            h = 2 * p + r
            hh = 2 * pq + r
            wt("vector", "cV", qevE[idx])
            if lnqE.get(2 * idx + r - 2) is not None:
                wt("vector", "cS", lnqE[2 * idx + r - 2])
            st("vector", lambda v, hh=hh, r=r, tsl=tsl: v.tensor_mul(
                sqk4b[0:64, r, 0:TQB], Qh[0:64, hh, tsl],
                Qh[0:64, hh, tsl]), "cV")
            wt("tensor", "cV")
            pn = pC if r == 0 else pD
            if lnqE.get(2 * idx + r - 2) is not None:
                wt("tensor", "cS", lnqE[2 * idx + r - 2])
            mm(pn[0:1, 0:TQB], ones128[0:64, :],
               sqk4b[0:64, r, 0:TQB], inc="cT")
            wt("scalar", "cT")
            st("scalar", lambda s, r=r, pn=pn: s.activation(
                st4b[:, 0:TQB], pn[0:1, 0:TQB], AF.Ln, bias=0.0,
                scale=1.0), "cS")
            wt("scalar", "cS")
            st("scalar", lambda s, hh=hh, tsl=tsl: s.activation(
                Qh[64:65, hh, tsl], st4b[:, 0:TQB], AF.Exp,
                bias=0.0, scale=0.5), "cS")
            lnqE[2 * idx + r] = cc("cS")[1]
            wt("vector", "cS", lnqE[2 * idx + r])
            st("vector", lambda v, hh=hh, h=h, tsl=tsl:
               v.tensor_scalar(Qh[64:65, hh, tsl],
                               Qh[64:65, hh, tsl],
                               knalr[:, h:h + 1], 30.0,
                               ALU.mult, ALU.add), "cV")
    for hg in range(2):
        for tq in range(NTQ):
            tsl = slice(tq * TQB, tq * TQB + TQB)
            tqa = hg * NTQ + tq
            xsl = tqa % 2
            drq = "drq%d" % xsl
            if tqa >= 2:
                wt("sync", "cT", qprojT[tqa - 2])
            dma("sync", x1q[:, xsl, :, :], x1sp[:, :, tsl], drq)
            wt("tensor", drq)
            if hg == 1 and tq == 0:
                wt("tensor", "cS", exp_done[-1])
            for pq in range(4):
                p = hg * 4 + pq
                idx = tqa * 4 + pq
                slot = scs4[idx % 4]
                if idx >= 4:
                    wt("tensor", "cV", qevE[idx - 4])
                for dc in range(DC):
                    mm(slot[:, 0:TQB], wiq_s[:, dc, 128 * p:128 * (p + 1)],
                       x1q[:, xsl, dc, :], first=(dc == 0), last=False)
                mm(slot[:, 0:TQB], wiqsum2[:, 128 * p:128 * (p + 1)],
                   negmu_own[:, tsl], first=False, last=True, inc="cT")
                qmmE = cc("cT")[1]
                wt("vector", "cT", qmmE)
                st("vector", lambda v, pq=pq, tsl=tsl, slot=slot:
                   v.tensor_tensor(Qh[0:64, 2 * pq, tsl], slot[0:64, 0:TQB],
                                   rstdb_all[0:64, tsl], ALU.mult), "cV")
                st("vector", lambda v, pq=pq, tsl=tsl, slot=slot:
                   v.tensor_tensor(Qh[0:64, 2 * pq + 1, tsl],
                                   slot[64:128, 0:TQB],
                                   rstdb_all[64:128, tsl], ALU.mult), "cV")
                wt("vector", "cV")
                for r in range(2):
                    h = 2 * p + r
                    hh = 2 * pq + r
                    st("vector", lambda v, hh=hh, h=h, tsl=tsl:
                       v.tensor_scalar_add(Qh[0:64, hh, tsl],
                                           Qh[0:64, hh, tsl],
                                           biq8h[:, h:h + 1]), "cV")
                qevE[idx] = cc("cV")[1]
                if qpend[0] is not None:
                    emit_qnorm(*qpend[0])
                qpend[0] = (p, pq, tsl, idx)
            qprojT[tqa] = cc("cT")[1]
        if qpend[0] is not None:
            emit_qnorm(*qpend[0])
            qpend[0] = None
        wt("tensor", "cV")
        wt("tensor", "cS")
        ng = (NKT + 1) // 2
        for pp in range(NPASS):
            ipass = hg * NPASS + pp
            sl = ipass % 2
            ksl = slice(pp * KQ, (pp + 1) * KQ)
            if ipass - 2 in attT:
                wt("sync", "cT", attT[ipass - 2])
            kldD[ipass] = dma("sync", Kpg[:, sl, :, :],
                              ksp[0:65, hg * 8:hg * 8 + 8, ksl],
                              "dkm%d" % sl)
            vldD[ipass] = dma("sync", Vpg[:, sl, :, :, :],
                              vsp[:, pp, :, hg * 8:hg * 8 + 8, :],
                              "dvm%d" % sl)
            wt("tensor", "dkm%d" % sl, kldD[ipass])
            wt("tensor", "dvm%d" % sl, vldD[ipass])
            NU = 8 * NQB
            prevu = [None]
            for uix in range(NU + 1):
                if uix < NU:
                    hh, qb = divmod(uix, NQB)
                    h = hg * 8 + hh
                    qsl = slice(qb * QB, qb * QB + QB)
                    gu = gcount[0]
                    gcount[0] += 1
                    scEv = []
                    for g in range(ng):
                        px = pA if g % 2 == 0 else pB
                        kts = [kt for kt in (2 * g, 2 * g + 1) if kt < NKT]
                        if gu >= 1:
                            wt("tensor", "cS", exp_done[2 * (gu - 1) + g])
                        for kt in kts:
                            mm(px[:, (kt % 2) * 512:(kt % 2) * 512 + QB],
                               Kpg[0:65, sl, hh, 128 * kt:128 * (kt + 1)],
                               Qh[0:65, hh, qsl],
                               inc="cT" if kt == kts[-1] else None)
                        scEv.append(cc("cT")[1])
                    for g in range(ng):
                        wt("scalar", "cT", scEv[g])
                        if gu >= 2:
                            wt("scalar", "cT", pv_all[gu - 2])
                        st("scalar", lambda s, g=g, gu=gu,
                           px=(pA if g % 2 == 0 else pB),
                           w=(len([kt for kt in (2 * g, 2 * g + 1)
                                   if kt < NKT]) - 1) * 512 + QB:
                           s.activation(
                               phat4[:, (2 * gu + g) % 4, 0:w], px[:, 0:w],
                               AF.Exp, bias=0.0, scale=1.0), "cS")
                        exp_done.append(cc("cS")[1])
                    prevu.append((gu, h, qsl))
                if uix >= 1:
                    gj, hj, qslj = prevu[-2] if uix < NU else prevu[-1]
                    hhj = hj - hg * 8
                    po = pC if gj % 2 == 0 else pD
                    if len(oev4) >= 2:
                        wt("tensor", "cV", oev4[-2])
                    for kt in range(NKT):
                        g = kt // 2
                        wt("tensor", "cS", exp_done[2 * gj + g])
                        mm(po[0:65, 0:QB], Vpg[:, sl, kt, hhj, :],
                           phat4[:, (2 * gj + g) % 4,
                                 (kt % 2) * 512:(kt % 2) * 512 + QB],
                           first=(kt == 0), last=(kt == NKT - 1),
                           inc="cT" if kt == NKT - 1 else None)
                    pv_all[gj] = cc("cT")[1]
                    wt("vector", "cT", pv_all[gj])
                    st("vector", lambda v, hj=hj, qslj=qslj, po=po:
                       v.tensor_tensor(o_acc[:, hj, qslj],
                                       o_acc[:, hj, qslj],
                                       po[0:65, 0:QB], ALU.add), "cV")
                    oev4.append(cc("cV")[1])
            attT[ipass] = cc("cT")[1]
            wt("tensor", "cV")
    bar()

    dma("gpsimd", wom_s[:], wom_h[:], "dwom")
    wt("tensor", "dwom")
    z5_used = []
    rzE5 = {}
    bcE5 = {}
    ttE5 = {}
    for tb in range(NTB5):
        tsl = slice(tb * TB5, (tb + 1) * TB5)
        drs = "dr5%d" % (tb % 2)
        if tb >= 1:
            wt("sync", "cV", z5_used[tb - 1])
        dma("sync", z5[:], x1sp[:, :, tsl], drs)
        dzs = "dz5%d" % (tb % 2)
        if tb >= 2:
            wt("sync", "cS", rzE5[tb - 2])
        for hz in range(H):
            dzv = dma("sync", zr16[hz:hz + 1, tb % 2, :],
                      o_acc[64:65, hz, tsl], dzs)
        wt("scalar", dzs, dzv)
        st("scalar", lambda s, tb=tb: s.activation(
            st16[:, :], zr16[:, tb % 2, :], AF.Ln, bias=0.0, scale=1.0),
           "cS")
        wt("scalar", "cS")
        if tb >= 2:
            wt("scalar", "cT", bcE5[(tb - 2, 7)])
        st("scalar", lambda s, tb=tb: s.activation(
            rz16[:, tb % 2, :], st16[:, :], AF.Exp, bias=0.0, scale=-1.0),
           "cS")
        rzE5[tb] = cc("cS")[1]
        for pr in range(HP):
            pze = scs4[2 * (pr % 2)][0:64, 0:TB5]
            pzo = scs4[2 * (pr % 2) + 1][0:64, 0:TB5]
            wt("tensor", "cS", rzE5[tb])
            if (tb, pr - 2) in ttE5:
                wt("tensor", "cV", ttE5[(tb, pr - 2)])
            elif (tb - 1, pr + 6) in ttE5:
                wt("tensor", "cV", ttE5[(tb - 1, pr + 6)])
            mm(pze, sel16[:, 2 * pr, :], rz16[:, tb % 2, :],
               first=True, last=True)
            mm(pzo, sel16[:, 2 * pr + 1, :], rz16[:, tb % 2, :],
               first=True, last=True, inc="cT")
            bcE5[(tb, pr)] = cc("cT")[1]
            wt("vector", "cT", bcE5[(tb, pr)])
            st("vector", lambda v, pr=pr, tsl=tsl, pze=pze:
               v.tensor_tensor(opn[0:64, pr, :], o_acc[0:64, 2 * pr, tsl],
                               pze, ALU.mult), "cV")
            st("vector", lambda v, pr=pr, tsl=tsl, pzo=pzo:
               v.tensor_tensor(opn[64:128, pr, :],
                               o_acc[0:64, 2 * pr + 1, tsl],
                               pzo, ALU.mult), "cV")
            ttE5[(tb, pr)] = cc("cV")[1]
        wt("tensor", "cV")
        wt("vector", drs)
        mm(pA[:, 0:TB5], onesr[:], negmu_own[:, tsl], inc="cT")
        wt("vector", "cT")
        for dc in range(DC):
            st("vector", lambda v, dc=dc: v.tensor_tensor(
                z5[:, dc, :], z5[:, dc, :], pA[:, 0:TB5], ALU.add), "cV")
        wt("tensor", "cV")
        mm(pA[:, 0:TB5], onesr[:], rstd_row[:, tsl], inc="cT")
        wt("vector", "cT")
        for dc in range(DC):
            st("vector", lambda v, dc=dc: v.tensor_tensor(
                z5[:, dc, :], z5[:, dc, :], pA[:, 0:TB5], ALU.mult), "cV")
        wt("tensor", "cV")
        for dc in range(DC):
            pwo = pC if dc % 2 == 0 else pD
            for pr in range(HP):
                mm(pwo[:, 0:TB5], wom_s[:, pr, 128 * dc:128 * (dc + 1)],
                   opn[:, pr, :], first=(pr == 0), last=(pr == HP - 1),
                   inc="cT" if pr == HP - 1 else None)
            wt("vector", "cT")
            st("vector", lambda v, dc=dc, tsl=tsl, pwo=pwo:
               v.scalar_tensor_tensor(
                   Wpre[:, dc, tsl], pwo[:, 0:TB5], bom_s[:, dc:dc + 1],
                   z5[:, dc, :], ALU.add, ALU.add), "cV")
            wt("tensor", "cV")
        z5_used.append(cc("cV")[1])
        for dc in range(DC):
            mm(pA[0:1, 0:TB5], ones128[:], Wpre[:, dc, tsl],
               first=(dc == 0), last=(dc == DC - 1),
               inc="cT" if dc == DC - 1 else None)
        for dc in range(DC):
            if dc >= 1:
                wt("vector", "cT")
            st("vector", lambda v, dc=dc, tsl=tsl: v.tensor_mul(
                sq5[:, :], Wpre[:, dc, tsl], Wpre[:, dc, tsl]), "cV")
            wt("tensor", "cV")
            mm(pD[0:1, 0:TB5], ones128[:], sq5[:, :], first=(dc == 0),
               last=(dc == DC - 1), inc="cT")
        wt("vector", "cT")
        st("vector", lambda v: v.tensor_scalar_mul(
            strow5[:, 0, :], pA[0:1, 0:TB5], -1.0 / D), "cV")
        st("vector", lambda v: v.tensor_scalar_mul(
            st5[:, 0, :], pD[0:1, 0:TB5], 1.0 / D), "cV")
        wt("vector", "cV")
        st("vector", lambda v: v.tensor_mul(
            st5[:, 1, :], strow5[:, 0, :], strow5[:, 0, :]), "cV")
        wt("vector", "cV")
        st("vector", lambda v: v.tensor_sub(
            st5[:, 0, :], st5[:, 0, :], st5[:, 1, :]), "cV")
        wt("vector", "cV")
        st("vector", lambda v: v.tensor_scalar_add(
            st5[:, 0, :], st5[:, 0, :], 1e-5), "cV")
        wt("scalar", "cV")
        st("scalar", lambda s: s.activation(
            st5[:, 1, :], st5[:, 0, :], AF.Ln, bias=0.0, scale=1.0), "cS")
        wt("scalar", "cS")
        st("scalar", lambda s: s.activation(
            strow5[:, 1, :], st5[:, 1, :], AF.Exp, bias=0.0, scale=-0.5),
           "cS")
        wt("tensor", "cS")
        mm(pA[:, 0:TB5], onesr[:], strow5[:, 0, :], inc="cT")
        mm(pD[:, 0:TB5], onesr[:], strow5[:, 1, :], inc="cT")
        wt("vector", "cT")
        for dc in range(DC):
            st("vector", lambda v, dc=dc, tsl=tsl: v.tensor_tensor(
                u_s[:, dc, tsl], Wpre[:, dc, tsl], pA[:, 0:TB5], ALU.add),
               "cV")
        wt("vector", "cV")
        for dc in range(DC):
            st("vector", lambda v, dc=dc, tsl=tsl: v.tensor_tensor(
                u_s[:, dc, tsl], u_s[:, dc, tsl], pD[:, 0:TB5], ALU.mult),
               "cV")
        wt("scalar", "cV")
        for dc in range(DC):
            st("scalar", lambda s, dc=dc, tsl=tsl: s.activation(
                u_s[:, dc, tsl], u_s[:, dc, tsl], AF.Identity,
                bias=ln1b_s[:, dc:dc + 1], scale=ln1g_s[:, dc:dc + 1]),
               "cS")
        wt("tensor", "cS")
        wt("vector", "cS")
    bar()

    for tt in range(NT16):
        for dc in range(DC):
            mm(pC[:, 0:4], u_s[:, dc, 128 * tt:128 * (tt + 1)],
               wgrp_s[:, dc, :], first=(dc == 0), last=(dc == DC - 1),
               inc="cT" if dc == DC - 1 else None)
        for dc in range(DC):
            mm(pC[:, 4:4 + E], u_s[:, dc, 128 * tt:128 * (tt + 1)],
               wexp_s[:, dc, :], first=(dc == 0), last=(dc == DC - 1),
               inc="cT" if dc == DC - 1 else None)
        wt("vector", "cT")
        st("vector", lambda v, tt=tt: v.tensor_tensor(
            gl_tm[:, tt, 0:4], pC[:, 0:4], bgb[:], ALU.add), "cV")
        st("vector", lambda v, tt=tt: v.tensor_tensor(
            gl_tm[:, tt, 4:12], pC[:, 4:4 + E], beb[:], ALU.add), "cV")
        wt("tensor", "cV")
    wt("vector", "cV")
    g4 = gl_tm.ap()[:, :, 0:4]
    ge_ = gl_tm.ap()[:, :, 4:12]

    def vv(fn):
        st("vector", fn, "cV")
        wt("vector", "cV")

    def ssx(fn):
        wt("scalar", "cV")
        st("scalar", fn, "cS")
        wt("vector", "cS")

    vv(lambda v: v.reduce_max(gsc[:, :, 0:1], g4, axis=AX, negate=True))
    vv(lambda v: v.tensor_tensor(gsc[:, :, 1:5], g4,
                                 gsc.ap()[:, :, 0:1].broadcast_to(
                                     (128, NT16, 4)), ALU.add))
    ssx(lambda s: s.activation(gsc[:, :, 1:5], gsc[:, :, 1:5], AF.Exp,
                               bias=0.0, scale=1.0))
    vv(lambda v: v.reduce_sum(gsc[:, :, 5:6], gsc[:, :, 1:5], axis=AX))
    vv(lambda v: v.reciprocal(gsc[:, :, 6:7], gsc[:, :, 5:6]))
    vv(lambda v: v.tensor_single_scalar(gsc[:, :, 7:11], gsc[:, :, 1:5],
                                        1.0, ALU.is_ge))
    vv(lambda v: v.tensor_tensor(gsc[:, :, 11:15], gsc[:, :, 7:11],
                                 ge_[:, :, 0:8:2], ALU.mult))
    vv(lambda v: v.reduce_sum(gsc[:, :, 15:16], gsc[:, :, 11:15], axis=AX))
    vv(lambda v: v.tensor_tensor(gsc[:, :, 11:15], gsc[:, :, 7:11],
                                 ge_[:, :, 1:8:2], ALU.mult))
    vv(lambda v: v.reduce_sum(gsc[:, :, 16:17], gsc[:, :, 11:15], axis=AX))
    vv(lambda v: v.tensor_max(gsc[:, :, 17:18], gsc[:, :, 15:16],
                              gsc[:, :, 16:17]))
    vv(lambda v: v.tensor_sub(gsc[:, :, 15:16], gsc[:, :, 15:16],
                              gsc[:, :, 17:18]))
    vv(lambda v: v.tensor_sub(gsc[:, :, 16:17], gsc[:, :, 16:17],
                              gsc[:, :, 17:18]))
    ssx(lambda s: s.activation(gsc[:, :, 15:17], gsc[:, :, 15:17], AF.Exp,
                               bias=0.0, scale=1.0))
    vv(lambda v: v.tensor_add(gsc[:, :, 17:18], gsc[:, :, 15:16],
                              gsc[:, :, 16:17]))
    vv(lambda v: v.reciprocal(gsc[:, :, 18:19], gsc[:, :, 17:18]))
    vv(lambda v: v.tensor_tensor(gsc[:, :, 18:19], gsc[:, :, 18:19],
                                 gsc[:, :, 6:7], ALU.mult))
    vv(lambda v: v.tensor_single_scalar(gsc[:, :, 15:17], gsc[:, :, 15:17],
                                        1.0, ALU.is_ge))
    vv(lambda v: v.tensor_tensor(gsc[:, :, 11:15], gsc[:, :, 7:11],
                                 gsc.ap()[:, :, 18:19].broadcast_to(
                                     (128, NT16, 4)), ALU.mult))
    vv(lambda v: v.tensor_tensor(c_tm.ap()[:, :, 0:E:2], gsc[:, :, 11:15],
                                 gsc.ap()[:, :, 15:16].broadcast_to(
                                     (128, NT16, 4)), ALU.mult))
    vv(lambda v: v.tensor_tensor(c_tm.ap()[:, :, 1:E:2], gsc[:, :, 11:15],
                                 gsc.ap()[:, :, 16:17].broadcast_to(
                                     (128, NT16, 4)), ALU.mult))
    wt("tensor", "cV")
    for tt in range(NT16):
        if tt >= 2:
            wt("tensor", "cV")
        pTx = pT if tt % 2 == 0 else pT2
        st("tensor", lambda t, tt=tt, pTx=pTx: t.transpose(
            pTx[0:E, 0:128], c_tm[:, tt, :], ident[:]), "cT")
        wt("vector", "cT")
        st("vector", lambda v, tt=tt, pTx=pTx: v.tensor_copy(
            c_fm[:, 128 * tt:128 * (tt + 1)], pTx[0:E, 0:128]), "cV")
    bar()

    w1D = {0: dma("gpsimd", w1b[0][:],
                  w1_h[0].rearrange("(c p) f -> p c f", p=128), "dw10")}
    w2D = {0: dma("gpsimd", w2_cur[:],
                  w2_h[0].rearrange("(c p) d -> p c d", p=128), "dw20")}
    yuse = []
    for e in range(E):
        wt("tensor", "dw1%d" % (e % 2), w1D[e])
        if e + 1 < E:
            if e >= 1:
                wt("gpsimd", "cT", yuse[e - 1])
            w1D[e + 1] = dma("gpsimd", w1b[(e + 1) % 2][:],
                             w1_h[e + 1].rearrange("(c p) f -> p c f",
                                                   p=128),
                             "dw1%d" % ((e + 1) % 2))
        for tb in range(NTB5):
            tsl = slice(tb * TB5, (tb + 1) * TB5)
            if e + tb > 0:
                wt("tensor", "cS")
            mm(pC[:, 0:TB5], sel8[:, e, :], c_fm[:, tsl], inc="cT")
            wt("vector", "cT")
            st("vector", lambda v: v.tensor_copy(cbc[:, 0:TB5],
                                                 pC[:, 0:TB5]), "cV")
            wt("tensor", "cV")
            hev = []
            for fc in range(FC):
                ph = pC if fc % 2 == 0 else pD
                if fc >= 2:
                    wt("tensor", "cV", hev[fc - 2])
                for dc in range(DC):
                    mm(ph[:, 0:TB5],
                       w1b[e % 2][:, dc, 128 * fc:128 * (fc + 1)],
                       u_s[:, dc, tsl], first=(dc == 0),
                       last=(dc == DC - 1),
                       inc="cT" if dc == DC - 1 else None)
                wt("scalar", "cT")
                st("scalar", lambda s, fc=fc, ph=ph, e=e: s.activation(
                    hsc[:, fc, :], ph[:, 0:TB5], AF.Relu,
                    bias=b1_s[:, e, fc:fc + 1], scale=1.0), "cS")
                wt("vector", "cS")
                st("vector", lambda v, fc=fc: v.tensor_tensor(
                    hsc[:, fc, :], hsc[:, fc, :], cbc[:, 0:TB5], ALU.mult),
                   "cV")
                hev.append(cc("cV")[1])
            wt("tensor", "cV")
            if tb == 0:
                wt("tensor", "dw2%d" % (e % 2), w2D[e])
            yev = []
            for dc in range(DC):
                py = pA if dc % 2 == 0 else pB
                if dc >= 2:
                    wt("tensor", "cV", yev[dc - 2])
                for fc in range(FC):
                    mm(py[:, 0:TB5], w2_cur[:, fc, 128 * dc:128 * (dc + 1)],
                       hsc[:, fc, :], first=(fc == 0), last=(fc == FC - 1),
                       inc="cT" if fc == FC - 1 else None)
                wt("vector", "cT")
                if e == 0:
                    st("vector", lambda v, dc=dc, tsl=tsl, py=py:
                       v.tensor_copy(y_acc[:, dc, tsl], py[:, 0:TB5]), "cV")
                else:
                    st("vector", lambda v, dc=dc, tsl=tsl, py=py:
                       v.tensor_tensor(y_acc[:, dc, tsl], y_acc[:, dc, tsl],
                                       py[:, 0:TB5], ALU.add), "cV")
                yev.append(cc("cV")[1])
            wt("tensor", "cV")
        yuse.append(cc("cT")[1])
        if e + 1 < E:
            wt("gpsimd", "cT", yuse[e])
            w2D[e + 1] = dma("gpsimd", w2_cur[:],
                             w2_h[e + 1].rearrange("(c p) d -> p c d",
                                                   p=128),
                             "dw2%d" % ((e + 1) % 2))
    bar()

    nstore = 0
    x1o_used = []
    for tb in range(NTB5):
        tsl = slice(tb * TB5, (tb + 1) * TB5)
        drs = "dr8%d" % (tb % 2)
        if tb >= 1:
            wt("sync", "cV", x1o_used[tb - 1])
        dma("sync", x1o8[:], x1sp[:, :, tsl], drs)
        for dc in range(DC):
            ps8 = pC if dc % 2 == 0 else pD
            mm(ps8[:, 0:TB5], b2_s[:, 128 * dc:128 * (dc + 1)],
               c_fm[:, tsl], inc="cT")
            wt("vector", "cT")
            st("vector", lambda v, dc=dc, tsl=tsl: v.tensor_tensor(
                y_acc[:, dc, tsl], y_acc[:, dc, tsl], u_s[:, dc, tsl],
                ALU.add), "cV")
            wt("vector", "cV")
            st("vector", lambda v, dc=dc, tsl=tsl, ps8=ps8: v.tensor_tensor(
                y_acc[:, dc, tsl], y_acc[:, dc, tsl], ps8[:, 0:TB5],
                ALU.add), "cV")
            wt("tensor", "cV")
        for dc in range(DC):
            mm(pA[0:1, 0:TB5], ones128[:], y_acc[:, dc, tsl],
               first=(dc == 0), last=(dc == DC - 1),
               inc="cT" if dc == DC - 1 else None)
        for dc in range(DC):
            if dc >= 1:
                wt("vector", "cT")
            st("vector", lambda v, dc=dc, tsl=tsl: v.tensor_mul(
                sq8[:, 0, :], y_acc[:, dc, tsl], y_acc[:, dc, tsl]), "cV")
            wt("tensor", "cV")
            mm(pD[0:1, 0:TB5], ones128[:], sq8[:, 0, :], first=(dc == 0),
               last=(dc == DC - 1), inc="cT")
        wt("vector", "cT")
        st("vector", lambda v: v.tensor_scalar_mul(
            strow8[:, 0, :], pA[0:1, 0:TB5], -1.0 / D), "cV")
        st("vector", lambda v: v.tensor_scalar_mul(
            st8[:, 0, :], pD[0:1, 0:TB5], 1.0 / D), "cV")
        wt("vector", "cV")
        st("vector", lambda v: v.tensor_mul(
            st8[:, 1, :], strow8[:, 0, :], strow8[:, 0, :]), "cV")
        wt("vector", "cV")
        st("vector", lambda v: v.tensor_sub(
            st8[:, 0, :], st8[:, 0, :], st8[:, 1, :]), "cV")
        wt("vector", "cV")
        st("vector", lambda v: v.tensor_scalar_add(
            st8[:, 0, :], st8[:, 0, :], 1e-5), "cV")
        wt("scalar", "cV")
        st("scalar", lambda s: s.activation(
            st8[:, 1, :], st8[:, 0, :], AF.Ln, bias=0.0, scale=1.0), "cS")
        wt("scalar", "cS")
        st("scalar", lambda s: s.activation(
            strow8[:, 1, :], st8[:, 1, :], AF.Exp, bias=0.0, scale=-0.5),
           "cS")
        wt("tensor", "cS")
        mm(pA[:, 0:TB5], onesr[:], strow8[:, 0, :], inc="cT")
        mm(pD[:, 0:TB5], onesr[:], strow8[:, 1, :], inc="cT")
        wt("vector", "cT")
        wt("vector", drs)
        for dc in range(DC):
            st("vector", lambda v, dc=dc, tsl=tsl: v.tensor_tensor(
                x2f[:, dc, :], y_acc[:, dc, tsl], pA[:, 0:TB5], ALU.add),
               "cV")
        wt("vector", "cV")
        for dc in range(DC):
            st("vector", lambda v, dc=dc: v.tensor_tensor(
                x2f[:, dc, :], x2f[:, dc, :], pD[:, 0:TB5], ALU.mult), "cV")
        wt("scalar", "cV")
        for dc in range(DC):
            st("scalar", lambda s, dc=dc: s.activation(
                x2f[:, dc, :], x2f[:, dc, :], AF.Identity,
                bias=ln2b_s[:, dc:dc + 1], scale=ln2g_s[:, dc:dc + 1]),
               "cS")
        wt("vector", "cS")
        for dc in range(DC):
            st("vector", lambda v, dc=dc: v.tensor_tensor(
                x2f[:, dc, :], x2f[:, dc, :], x1o8[:, dc, :], ALU.add), "cV")
        x1o_used.append(cc("cV")[1])
        wt("tensor", "cV")
        if tb == 0:
            trE8, cpE8, g8c = {}, {}, [0]
        for ts_ in range(TB5 // 128):
            slot = nstore % 2
            if nstore >= 2:
                wt("vector", "dout%d" % slot, 16 * (nstore // 2))
            qslots = (pA[:, 0:128], pA[:, 512:512 + 128],
                      pB[:, 0:128], pB[:, 512:512 + 128])
            for dc in range(DC + 1):
                if dc < DC:
                    g8 = g8c[0]
                    g8c[0] += 1
                    pxv = qslots[g8 % 4]
                    if g8 - 4 in cpE8:
                        wt("tensor", "cV", cpE8[g8 - 4])
                    st("tensor", lambda t, dc=dc, ts_=ts_, pxv=pxv:
                       t.transpose(pxv,
                                   x2f[:, dc, 128 * ts_:128 * (ts_ + 1)],
                                   identf[:]), "cT")
                    trE8[g8] = cc("cT")[1]
                if dc >= 1:
                    j = dc - 1
                    gidx = g8c[0] - 2 if dc < DC else g8c[0] - 1
                    wt("vector", "cT", trE8[gidx])
                    st("vector", lambda v, j=j, slot=slot,
                       pxv=qslots[gidx % 4]:
                       v.tensor_copy(x2tm[:, slot, 128 * j:128 * (j + 1)],
                                     pxv), "cV")
                    cpE8[gidx] = cc("cV")[1]
            wt("sync", "cV")
            wt("tensor", "cV")
            row0 = tb * TB5 + ts_ * 128
            dma("sync", out_h[row0:row0 + 128, :], x2tm[:, slot, :],
                "dout%d" % slot)
            nstore += 1
    wt("sync", "dout0", 16 * ((nstore + 1) // 2))
    wt("sync", "dout1", 16 * (nstore // 2))

    with nc.allow_low_precision(reason="bf16 activation pipeline"), \
            nc.Block() as block:
        for e in ENGINES:
            fns = Q[e]

            def run(eng, fns=fns):
                for fn in fns:
                    fn(eng)
            getattr(block, e)(run)
    stack.close()
    return nc


_BUILT = {}


def _get_program(T):
    if T not in _BUILT:
        _BUILT[T] = build_program(make_cfg(T))
    return _BUILT[T]


def _prep(inputs, cfg):
    T, TO, D, H, HD, C, E, FC, DC = (cfg["T"], cfg["TO"], cfg["D"], cfg["H"],
                                     cfg["HD"], cfg["C"], cfg["E"], cfg["FC"],
                                     cfg["DC"])
    f32 = np.float32
    g = {k: np.asarray(v, dtype=f32) for k, v in inputs.items()}
    x = g["L0_attn_x"]
    B = x.shape[0]

    def col(vec):
        return np.ascontiguousarray(vec.reshape(DC, 128).T)

    shared = {
        "wq": np.ascontiguousarray(g["L0_attn_Wq"]),
        "wk": np.ascontiguousarray(g["L0_attn_Wk"]),
        "wv": np.ascontiguousarray(
            g["L0_attn_Wv"].reshape(D, H // 2, 2, HD).transpose(
                0, 2, 1, 3).reshape(D, D)),
        "wo_r": np.ascontiguousarray(
            g["L0_attn_Wo"].reshape(H // 2, 128, D).transpose(1, 0, 2)),
        "bq": col(g["L0_attn_bq"]), "bk": col(g["L0_attn_bk"]),
        "bv_row": np.ascontiguousarray(
            g["L0_attn_bv"].reshape(H // 2, 2, HD).transpose(
                1, 0, 2).reshape(1, D)),
        "bo": col(g["L0_attn_bo"]),
        "wiq": np.ascontiguousarray(g["L0_ffn_in_proj_w"][0:D].T),
        "wik": np.ascontiguousarray(g["L0_ffn_in_proj_w"][D:2 * D].T),
        "wiv": np.ascontiguousarray(g["L0_ffn_in_proj_w"][2 * D:3 * D].T),
        "biq": col(g["L0_ffn_in_proj_b"][0:D]),
        "bik": col(g["L0_ffn_in_proj_b"][D:2 * D]),
        "biq8h": np.ascontiguousarray(
            g["L0_ffn_in_proj_b"][0:D].reshape(H, HD).T),
        "bikh": np.ascontiguousarray(
            g["L0_ffn_in_proj_b"][D:2 * D].reshape(H, HD).T),
        "biv_row": np.ascontiguousarray(
            g["L0_ffn_in_proj_b"][2 * D:3 * D].reshape(1, D)),
        "wo_mha_r": np.ascontiguousarray(
            g["L0_ffn_out_proj_w"].T.reshape(H // 2, 128, D).transpose(
                1, 0, 2)),
        "bo_mha": col(g["L0_ffn_out_proj_b"]),
        "ln1g": col(g["L0_ffn_ln1_g"]), "ln1b": col(g["L0_ffn_ln1_b"]),
        "ln2g": col(g["L0_ffn_ln2_g"]), "ln2b": col(g["L0_ffn_ln2_b"]),
        "wgrp": np.ascontiguousarray(g["L0_ffn_w_group"]),
        "bgrp_row": np.ascontiguousarray(g["L0_ffn_b_group"].reshape(1, 4)),
        "wexp": np.ascontiguousarray(
            g["L0_ffn_w_expert"].transpose(1, 0, 2).reshape(D, E)),
        "bexp_row": np.ascontiguousarray(
            g["L0_ffn_b_expert"].reshape(1, E)),
        "w1": np.ascontiguousarray(g["L0_ffn_w1"]),
        "b1_r": np.ascontiguousarray(
            g["L0_ffn_b1"].reshape(E, FC, 128).transpose(0, 2, 1)),
        "w2": np.ascontiguousarray(g["L0_ffn_w2"]),
        "b2": np.ascontiguousarray(g["L0_ffn_b2"]),
        "ident": np.eye(128, dtype=f32),
        "ones_row": np.ones((1, 128), f32),
        "sel8": np.ascontiguousarray(
            np.kron(np.eye(E, dtype=f32), np.ones((1, 128), f32)).reshape(
                E, E * 128)),
        "ones_col": np.ones((128, 1), f32),
        "sel2": np.ascontiguousarray(
            np.kron(np.eye(2, dtype=f32), np.ones((64, 1), f32))),
        "sel2t": np.ascontiguousarray(
            np.kron(np.eye(2, dtype=f32), np.ones((1, 64), f32))),
        "sel16": np.ascontiguousarray(
            np.kron(np.eye(16, dtype=f32), np.ones((1, 64), f32))),
    }
    ends = np.arange(0, T, 64, dtype=np.int64)
    starts = np.clip(ends - 63, 0, None)
    lens = (ends - starts + 1).astype(f32)
    aggA = np.zeros((T, C), f32)
    for ci in range(C):
        aggA[starts[ci]:ends[ci] + 1, ci] = 1.0 / lens[ci]
    allowed = (ends[:, None] <= np.arange(T)[None, :]).astype(f32)

    in_maps = []
    for core in range(2 * B):
        b, hh = core // 2, core % 2
        perm = np.concatenate([np.arange(hh * TO, (hh + 1) * TO),
                               np.arange((1 - hh) * TO, (2 - hh) * TO)])
        m = dict(shared)
        m["x"] = np.ascontiguousarray(x[b][perm])
        m["aggA"] = np.ascontiguousarray(aggA[perm])
        mq = np.zeros((128, T), f32)
        mq[0:C] = allowed[:, perm]
        mq[64:64 + C] = allowed[:, perm]
        m["maskq"] = mq
        in_maps.append(m)
    return in_maps


def kernel(**inputs):
    T = int(np.asarray(inputs["L0_attn_x"]).shape[1])
    cfg = make_cfg(T)
    nc = _get_program(T)
    in_maps = _prep(inputs, cfg)
    B = len(in_maps) // 2
    res = run_bass_kernel_spmd(nc, in_maps, core_ids=list(range(len(in_maps))))
    TO, D = cfg["TO"], cfg["D"]
    out = np.empty((B, T, D), np.float32)
    for core, r in enumerate(res.results):
        b, hh = core // 2, core % 2
        out[b, hh * TO:(hh + 1) * TO, :] = r["out"]
    return out
